# version 01
# baseline (speedup 1.0000x reference)
import contextlib
import numpy as np
import ml_dtypes
import concourse.bass as bass
import concourse.mybir as mybir
from concourse.bass_utils import run_bass_kernel_spmd
from concourse.aot_env import allow_ici_all_to_all

F32 = mybir.dt.float32
BF16 = mybir.dt.bfloat16
F16 = mybir.dt.float16
I32 = mybir.dt.int32
U16 = mybir.dt.uint16
AF = mybir.ActivationFunctionType
ALU = mybir.AluOpType

NP_BF16 = ml_dtypes.bfloat16

D = 2048
T = 2048
B = 4
TPC = 1024
HID = 8192
I_FFN = 5632
IPC = I_FFN // 8
IPP = IPC // 8
IPC2 = 768
IPP2 = 96
LN_EPS = 1e-5
SC_EPS = 1e-6

PI = np.array([8 * hp + j for j in range(8) for hp in range(256)], dtype=np.int64)
GATE_PERM = np.array([h * 16 + q * 4 + tl for h in range(2)
                      for q in (1, 0, 3, 2) for tl in range(4)], dtype=np.int64)
PAIRS = [[0, 1], [2, 3], [4, 5], [6, 7]]
ALL8 = [list(range(8))]
ENGINES = ("sync", "tensor", "vector", "scalar", "gpsimd")



def _hadamard(n):
    h = np.array([[1.0]], dtype=np.float64)
    while h.shape[0] < n:
        h = np.block([[h, h], [h, -h]])
    return h


def tile_lhsT(wT, mo=128):
    K, M = wT.shape
    a = np.ascontiguousarray(wT, dtype=np.float32).reshape(K // 128, 128, M // mo, mo)
    return np.ascontiguousarray(a.transpose(2, 0, 1, 3))


def col_tiles(v):
    return np.ascontiguousarray(np.asarray(v, np.float32).reshape(-1, 128).T)


def prep_core(inputs, c):
    b, s = c // 2, c % 2
    tsl = slice(s * TPC, (s + 1) * TPC)
    f32 = np.float32
    out = {}

    x = np.asarray(inputs["L0_attn_x"])
    out["x_c"] = np.ascontiguousarray(x[b, tsl, :][:, PI].T.astype(f32))

    out["w_inT"] = tile_lhsT(np.asarray(inputs["L0_attn_W_in"]).T[PI, :])
    my_rows = slice(2 * s * 2048, (2 * s + 2) * 2048)
    wgt = tile_lhsT(np.asarray(inputs["L0_attn_W_gate"])[my_rows, :].T)
    bgt = col_tiles(np.asarray(inputs["L0_attn_b_gate"])[my_rows])
    out["w_gateT"] = np.ascontiguousarray(wgt[GATE_PERM])
    out["b_gate"] = np.ascontiguousarray(bgt[:, GATE_PERM])
    out["rms_w"] = col_tiles(inputs["L0_attn_rms_w"])
    out["w_outT"] = tile_lhsT(np.asarray(inputs["L0_attn_W_out"]).T[:, PI])

    out["w1T"] = tile_lhsT(np.asarray(inputs["L0_ffn_W1"]).T[PI, :])
    out["w3T"] = tile_lhsT(np.asarray(inputs["L0_ffn_W3"]).T[PI, :])
    w2T = np.asarray(inputs["L0_ffn_W2"]).T[:, PI].astype(f32)
    w2t_tiles = np.empty((16, 64, 128, 128), dtype=f32)
    alpha = np.arange(64)
    for beta in range(64):
        rows = np.concatenate([alpha * 128 + beta, alpha * 128 + beta + 64])
        w2t_tiles[:, beta] = w2T[rows, :].reshape(128, 16, 128).transpose(1, 0, 2)
    out["w2T"] = np.ascontiguousarray(w2t_tiles.reshape(32, 32, 128, 128))

    csl = slice(s * 1024, (s + 1) * 1024)
    for nm in ("Wk", "Wv", "Wr"):
        w = np.asarray(inputs[f"L1_attn_{nm}"])
        out[nm.lower() + "T"] = tile_lhsT(w.T[PI, :][:, csl])
    for nm in ("bk", "bv", "br"):
        out[nm] = col_tiles(np.asarray(inputs[f"L1_attn_{nm}"])[csl])
    out["woT"] = tile_lhsT(np.asarray(inputs["L1_attn_Wo"]).T[:, PI])
    out["bo"] = col_tiles(np.asarray(inputs["L1_attn_bo"])[PI])
    for nm in ("tmk", "tmv", "tmr"):
        out[nm] = col_tiles(np.asarray(inputs[f"L1_attn_{nm}"]).reshape(-1)[PI])
    out["td"] = col_tiles(np.asarray(inputs["L1_attn_time_decay"])[csl])
    out["tf"] = col_tiles(np.asarray(inputs["L1_attn_time_first"])[csl])

    p_idx = np.arange(IPC2)
    jj, ip = p_idx // IPP2, p_idx % IPP2
    real = ip < IPP
    chan = np.where(real, c * IPC + 8 * np.minimum(ip, IPP - 1) + jj, 0)
    gup = np.asarray(inputs["L1_ffn_gate_up_packed"])
    gus = np.asarray(inputs["L1_ffn_gate_up_scales"])
    gu_pkT = np.zeros((256, 2 * IPC2), np.int32)
    gu_scT = np.zeros((256, 2 * IPC2), np.float16)
    gcols = gup[chan, :].T.astype(np.int32)
    ucols = gup[I_FFN + chan, :].T.astype(np.int32)
    gu_pkT[:, 0:IPC2] = np.where(real[None, :], gcols, 0)
    gu_pkT[:, IPC2:] = np.where(real[None, :], ucols, 0)
    gsc = gus[:, chan][np.arange(256) // 16, :].astype(np.float16)
    usc = gus[:, I_FFN + chan][np.arange(256) // 16, :].astype(np.float16)
    gu_scT[:, 0:IPC2] = np.where(real[None, :], gsc, np.float16(0))
    gu_scT[:, IPC2:] = np.where(real[None, :], usc, np.float16(0))
    out["gu_pkT"] = np.ascontiguousarray(gu_pkT)
    out["gu_scT"] = np.ascontiguousarray(gu_scT)
    dpk = np.asarray(inputs["L1_ffn_down_packed"])
    dns = np.asarray(inputs["L1_ffn_down_scales"])
    dn_pkT = np.zeros((IPP2, 2048), np.int32)
    dn_scT = np.zeros((IPP2, 2048), np.float16)
    mypc = slice(c * IPP, (c + 1) * IPP)
    dn_pkT[0:IPP] = dpk[PI, :][:, mypc].T.astype(np.int32)
    g_ip = (c * IPC + 8 * np.arange(IPP)) // 128
    dn_scT[0:IPP] = dns[:, PI][g_ip, :].astype(np.float16)
    out["dn_pkT"] = np.ascontiguousarray(dn_pkT)
    out["dn_scT"] = np.ascontiguousarray(dn_scT)

    m_eo = np.zeros((128, 2), np.float32)
    m_eo[:, s] = 1.0
    out["m_eo"] = m_eo
    out["zcol"] = np.zeros((128, 16), NP_BF16)

    out["h128"] = np.ascontiguousarray(_hadamard(128).astype(NP_BF16))
    h64 = _hadamard(64).astype(NP_BF16)
    out["h64x2"] = np.ascontiguousarray(np.concatenate([h64, h64], axis=0))
    return out


def unpermute_out(out_pos):
    o = np.empty_like(out_pos)
    o[PI, :] = out_pos
    return o.T



class Ctx:
    def __init__(self, nc, es):
        self.nc = nc
        self.es = es
        self.streams = {e: [] for e in ENGINES}
        self._sems = {}
        self.sb = {}
        self.psum = []
        self.bank_guard = {}

    def sem(self, name):
        if name not in self._sems:
            h = self.es.enter_context(self.nc.semaphore(name))
            self._sems[name] = [h, 0]
        return self._sems[name]

    def count(self, name):
        return self.sem(name)[1]

    def op(self, eng, meth, *a, inc=(), **kw):
        pairs = []
        for (s, k) in inc:
            rec = self.sem(s)
            rec[1] += k
            pairs.append((rec[0], k))

        def fn(e, meth=meth, a=a, kw=kw, pairs=pairs):
            inst = getattr(e, meth)(*a, **kw)
            for (h, k) in pairs:
                inst.then_inc(h, k)
        self.streams[eng].append(fn)
        return self.count(inc[0][0]) if inc else None

    def wait(self, eng, s, val=None):
        rec = self.sem(s)
        v = rec[1] if val is None else val
        self.streams[eng].append(lambda e, h=rec[0], v=v: e.wait_ge(h, v))
        return v

    def dma(self, eng, out, in_, s, **kw):
        return self.op(eng, "dma_start", inc=((s, 16),), out=out, in_=in_, **kw)

    def alloc_sb(self, name, shape, dtype):
        if name in self.sb:
            return self.sb[name]
        h = self.nc.alloc_sbuf_tensor(name, shape, dtype)
        self.sb[name] = h
        return h

    def alloc_psum_banks(self, n=8):
        for i in range(n):
            self.psum.append(self.nc.alloc_psum_tensor(f"pbank{i}", [128, 512], F32))

    def use_bank(self, i):
        g = self.bank_guard.get(i)
        if g is not None:
            self.wait("tensor", g[0], g[1])
            self.bank_guard[i] = None

    def guard_bank(self, i, semname):
        self.bank_guard[i] = (semname, self.count(semname))

    def finalize(self, block):
        for eng in ENGINES:
            deco = getattr(block, eng)
            fns = self.streams[eng]

            def runner(e, fns=fns):
                for f in fns:
                    f(e)
            deco(runner)


class WStream:

    def __init__(self, ctx, name, dram_ap, nob, nkc, mo=128, qeng="gpsimd", buf=None):
        self.ctx = ctx
        self.dram = dram_ap
        self.nob, self.nkc, self.mo = nob, nkc, mo
        self.buf = buf if buf is not None else ctx.alloc_sb(f"ws_{name}", [128, 2 * nkc * mo], BF16)
        self.sld = f"wld_{name}"
        self.sfr = f"wfr_{name}"
        ctx.sem(self.sld)
        ctx.sem(self.sfr)
        self.qeng = qeng

    def load(self, ob, seq=None, wait=None):
        ctx = self.ctx
        seq = ob if seq is None else seq
        if wait is not None:
            ctx.wait(self.qeng, wait[0], wait[1])
        dst = self.buf[:][:, 0:2 * self.nkc * self.mo].rearrange(
            "p (s k m) -> p s k m", s=2, k=self.nkc)[:, seq % 2]
        src = self.dram[ob].rearrange("k p m -> p k m")
        ctx.dma(self.qeng, dst, src, self.sld)

    def lhsT(self, seq, kc):
        off = (seq % 2) * self.nkc * self.mo + kc * self.mo
        return self.buf[:, off:off + self.mo]

    def wait_ob(self, seq):
        self.ctx.wait("tensor", self.sld, 16 * (seq + 1))


def mm_accum(ctx, bank, ws, seq, rhs_fn, nslice, extra_inc=()):
    pb = ctx.psum[bank]
    w = nslice.stop - nslice.start
    nkc = ws.nkc
    for kc in range(nkc):
        last = kc == nkc - 1
        ctx.op("tensor", "matmul", pb[:, 0:w], ws.lhsT(seq, kc),
               rhs_fn(kc)[:, nslice], start=(kc == 0), stop=last,
               inc=(extra_inc[:1] if last else ()))
    for pair in extra_inc[1:]:
        ctx.op("tensor", "nop", inc=(pair,))



def emit_layernorm(ctx, sx, out_bf, tag, nt=16, tw=1024, inv_n=None):
    inv_n = 1.0 / (nt * 128) if inv_n is None else inv_n
    s_sq = ctx.sb["s_sq"]
    sqbf = s_sq[:].bitcast(BF16)
    s_stat = ctx.sb["s_stat"]
    s_m = ctx.sb["s_m"]
    s_rstd = ctx.sb["s_rstd"]
    s_tsub = ctx.sb["s_sq"]
    oc16 = ctx.sb["ones_col16"]
    or16 = ctx.sb["ones_row16"]
    nn = tw // 512

    sxt = sx[:].rearrange("p (i t) -> p i t", i=nt)
    obt = out_bf[:].rearrange("p (i t) -> p i t", i=nt)

    for i in range(nt):
        ctx.op("vector", "tensor_copy", out=obt[:, i], in_=sxt[:, i],
               inc=((f"{tag}_cast", 1),))
    for i in range(nt):
        if i >= 2:
            ctx.wait("scalar", f"{tag}_sqc", i - 1)
        ctx.wait("scalar", f"{tag}_cast", i + 1)
        ctx.op("scalar", "activation", out=sqbf[:, (i % 2) * tw:(i % 2 + 1) * tw],
               in_=obt[:, i], func=AF.Square, inc=((f"{tag}_sq", 1),))
    for bk in range(4):
        ctx.use_bank(bk)
    ctx.wait("tensor", "vinit", 1)
    for i in range(nt):
        ctx.wait("tensor", f"{tag}_cast", i + 1)
        for n in range(nn):
            ctx.op("tensor", "matmul", ctx.psum[n][0:1, 0:512], oc16[:, 0:1],
                   obt[:, i][:, n * 512:(n + 1) * 512],
                   start=(i == 0), stop=(i == nt - 1),
                   inc=(((f"{tag}_smm", 1),) if i == nt - 1 else ()))
        ctx.wait("tensor", f"{tag}_sq", i + 1)
        for n in range(nn):
            last_n = n == nn - 1
            inc = ()
            if last_n:
                inc = ((f"{tag}_smm", 1),) if i == nt - 1 else ((f"{tag}_sqc", 1),)
            ctx.op("tensor", "matmul", ctx.psum[2 + n][0:1, 0:512], oc16[:, 0:1],
                   sqbf[:, (i % 2) * tw + n * 512:(i % 2) * tw + (n + 1) * 512],
                   start=(i == 0), stop=(i == nt - 1), inc=inc)
    ctx.wait("vector", f"{tag}_smm")
    for n in range(nn):
        ctx.op("vector", "tensor_copy", out=s_stat[0:1, n * 512:(n + 1) * 512],
               in_=ctx.psum[n][0:1, 0:512])
        ctx.op("vector", "tensor_copy", out=s_stat[0:1, tw + n * 512:tw + (n + 1) * 512],
               in_=ctx.psum[2 + n][0:1, 0:512], inc=((f"{tag}_sev", 1),))
    ctx.wait("tensor", f"{tag}_sev")
    for n in range(nn):
        ctx.op("tensor", "matmul", ctx.psum[n][:, 0:512], ctx.sb["ones_row"][0:1, 0:128],
               s_stat[0:1, n * 512:(n + 1) * 512], start=True, stop=True)
        ctx.op("tensor", "matmul", ctx.psum[2 + n][:, 0:512], ctx.sb["ones_row"][0:1, 0:128],
               s_stat[0:1, tw + n * 512:tw + (n + 1) * 512], start=True, stop=True,
               inc=((f"{tag}_bc", 1),))
    ctx.wait("vector", f"{tag}_bc")
    for n in range(nn):
        ctx.op("vector", "tensor_scalar", out=s_m[:, n * 512:(n + 1) * 512],
               in0=ctx.psum[n][:, 0:512], scalar1=float(inv_n), scalar2=None,
               op0=ALU.mult, inc=((f"{tag}_m", 1),))
    ctx.wait("scalar", f"{tag}_m")
    ctx.op("scalar", "activation", out=s_sq[:, 0:tw], in_=s_m[:, 0:tw],
           func=AF.Square, inc=((f"{tag}_m2", 1),))
    ctx.wait("vector", f"{tag}_m2")
    for n in range(nn):
        ctx.op("vector", "scalar_tensor_tensor", out=s_sq[:, tw + n * 512:tw + (n + 1) * 512],
               in0=ctx.psum[2 + n][:, 0:512], scalar=float(inv_n),
               in1=s_sq[:, n * 512:(n + 1) * 512], op0=ALU.mult, op1=ALU.subtract,
               inc=((f"{tag}_var", 1),))
    for bk in range(4):
        ctx.guard_bank(bk, f"{tag}_var")
    ctx.wait("scalar", f"{tag}_var")
    ctx.op("scalar", "activation", out=s_sq[:, 0:tw], in_=s_sq[:, tw:2 * tw],
           func=AF.Ln, bias=ctx.sb["s_eps5"][:, 0:1], scale=1.0)
    ctx.op("scalar", "activation", out=s_rstd[:, 0:tw], in_=s_sq[:, 0:tw],
           func=AF.Exp, bias=0.0, scale=-0.5, inc=((f"{tag}_rs", 1),))
    ctx.wait("gpsimd", f"{tag}_rs")
    ctx.wait("vector", f"{tag}_rs")
    for i in range(nt):
        if i >= 2:
            ctx.wait("vector", f"{tag}_mul", i - 1)
        ctx.op("vector", "tensor_sub", out=s_tsub[:, (i % 2) * tw:(i % 2 + 1) * tw],
               in0=sxt[:, i], in1=s_m[:, 0:tw], inc=((f"{tag}_sub", 1),))
        ctx.wait("gpsimd", f"{tag}_sub", i + 1)
        ctx.op("gpsimd", "tensor_mul", out=obt[:, i],
               in0=s_tsub[:, (i % 2) * tw:(i % 2 + 1) * tw], in1=s_rstd[:, 0:tw],
               inc=((f"{tag}_mul", 1),))
    return f"{tag}_mul", ctx.count(f"{tag}_mul")



def ph_l0attn(ctx, ap):
    sx = ctx.sb["sx"]
    pbA = ctx.sb["pbA"]

    ctx.dma("sync", ctx.sb["s_bgate"][:], ap["b_gate"], "cld")
    ctx.dma("sync", ctx.sb["s_rms"][:], ap["rms_w"], "cld")
    ctx.dma("sync", ctx.sb["s_meo"][:], ap["m_eo"], "cld")
    cld_t = ctx.count("cld")

    lnsem, lncnt = emit_layernorm(ctx, sx, pbA, "ln1")

    ws_in = WStream(ctx, "w_in", ap["w_inT"], 16, 16)
    s_xp = ctx.sb["s_stage"]
    ws_in.load(0)
    ws_in.load(1)
    xlnt = lambda kc: pbA[:, kc * 1024:(kc + 1) * 1024]
    ctx.wait("tensor", lnsem, lncnt)
    for ob in range(16):
        ws_in.wait_ob(ob)
        for n in range(2):
            bank = (2 * ob + n) % 4
            ctx.use_bank(bank)
            mm_accum(ctx, bank, ws_in, ob, xlnt, slice(n * 512, (n + 1) * 512),
                     extra_inc=(("mmxp", 1),))
            ctx.wait("vector", "mmxp")
            if n == 0 and ob >= 2 and ob % 2 == 0:
                ctx.wait("vector", "agw", 16 * (ob // 2))
            ctx.op("vector", "tensor_copy",
                   out=s_xp[:, (ob % 2) * 1024 + n * 512:(ob % 2) * 1024 + (n + 1) * 512],
                   in_=ctx.psum[bank][:, 0:512], inc=(("evxp", 1),))
            ctx.guard_bank(bank, "evxp")
        if ob + 2 < 16:
            ws_in.load(ob + 2, wait=("evxp", ctx.count("evxp")))
        if ob % 2 == 1:
            ctx.wait("gpsimd", "evxp")
            ctx.dma("gpsimd",
                    ap["ag_i"].rearrange("(g o p) t -> g p o t", o=2, p=128)[ob // 2],
                    s_xp[:].rearrange("p (o t) -> p o t", o=2), "agw")

    ctx.wait("gpsimd", "agw")
    ctx.op("gpsimd", "collective_compute", "AllGather", ALU.bypass,
           replica_groups=PAIRS, ins=[ap["ag_i"].opt()], outs=[ap["ag_o"].opt()],
           inc=(("ccs", 1),))

    ws_g = WStream(ctx, "w_gate", ap["w_gateT"], 32, 16)
    s_xpf = pbA
    s_ga, s_gb = ctx.sb["s_ga"], ctx.sb["s_gb"]
    s_gc = ctx.sb["s_gc"]
    s_gd = s_gb
    s_carry = ctx.sb["s_carry"]
    s_hst = ctx.sb["s_hstage"]
    s_sq = ctx.sb["s_sq"]
    sqbf = s_sq[:].bitcast(BF16)
    s_stat = ctx.sb["s_stat"]
    s_rstd = ctx.sb["s_rstd"]
    ctx.op("vector", "memset", s_carry[:], 0.0)
    ctx.wait("scalar", "cld", cld_t)

    ccs_t = ctx.count("ccs")
    for chunk in range(2):
        ctx.wait("sync", "ccs", ccs_t)
        if chunk > 0:
            ctx.wait("sync", "mmg")
        blk = ap["ag_o"].rearrange("(g k p) t -> g k p t", k=16, p=128)
        ctx.dma("sync", s_xpf[:].rearrange("p (k t) -> p k t", k=16),
                blk[chunk].rearrange("k p t -> p k t"), "xpfld")
        xpf_t = ctx.count("xpfld")
        xpft = lambda kc: s_xpf[:, kc * 1024:(kc + 1) * 1024]
        if chunk == 0:
            ctx.wait("tensor", lnsem, lncnt)
        ctx.wait("tensor", "xpfld", xpf_t)

        if chunk == 0:
            ws_g.load(0, seq=0)
            ws_g.load(1, seq=1)
        for head in range(2):
            oh = head * 16
            ctx.wait("scalar", "hok")
            for (pos, func, tgt) in ((0, AF.Sigmoid, s_ga), (1, AF.Exp, s_gb),
                                     (2, AF.Tanh, s_gc)):
                for tl in range(4):
                    ob = oh + pos * 4 + tl
                    seq = ob + 32 * chunk
                    ws_g.wait_ob(seq)
                    for n in range(2):
                        bank = (2 * tl + n) % 4
                        ctx.use_bank(bank)
                        mm_accum(ctx, bank, ws_g, seq, xpft,
                                 slice(n * 512, (n + 1) * 512),
                                 extra_inc=(("mmg", 1),))
                        ctx.wait("scalar", "mmg")
                        ctx.op("scalar", "activation",
                               out=tgt[:, tl * 1024 + n * 512: tl * 1024 + (n + 1) * 512],
                               in_=ctx.psum[bank][:, 0:512], func=func,
                               bias=ctx.sb["s_bgate"][:, ob:ob + 1], scale=1.0,
                               inc=(("evg", 1),))
                        ctx.guard_bank(bank, "evg")
                    if seq + 2 < 64:
                        ws_g.load((ob + 2) % 32, seq=seq + 2,
                                  wait=("evg", ctx.count("evg")))
            ctx.wait("vector", "evg")
            ctx.wait("vector", "scn")
            for tl in range(4):
                sl = slice(tl * 1024, (tl + 1) * 1024)
                cc = head * 4 + tl
                ctx.op("vector", "tensor_mul", out=s_gb[:, sl], in0=s_gb[:, sl],
                       in1=s_gc[:, sl])
                ctx.op("vector", "tensor_tensor_scan", out=s_gc[:, sl],
                       data0=s_ga[:, sl], data1=s_gb[:, sl],
                       initial=s_carry[:, cc:cc + 1], op0=ALU.mult, op1=ALU.add,
                       inc=(("scnv", 1),))
            ctx.wait("scalar", "scnv")
            for tl in range(4):
                cc = head * 4 + tl
                ctx.op("scalar", "copy", out=s_carry[:, cc:cc + 1],
                       in_=s_gc[:, tl * 1024 + 1023: tl * 1024 + 1024],
                       inc=(("scn", 1),))
            ctx.wait("scalar", "scn")
            for tl in range(4):
                ctx.op("scalar", "activation", out=sqbf[:, tl * 1024:(tl + 1) * 1024],
                       in_=s_gc[:, tl * 1024:(tl + 1) * 1024], func=AF.Square,
                       inc=(("csq", 1),))
            ctx.wait("tensor", "csq")
            ctx.use_bank(4)
            ctx.use_bank(5)
            for n in range(2):
                for tl in range(4):
                    ctx.op("tensor", "matmul", ctx.psum[4 + n][0:1, 0:512],
                           ctx.sb["ones_col16"][:, 0:1],
                           sqbf[:, tl * 1024 + n * 512: tl * 1024 + (n + 1) * 512],
                           start=(tl == 0), stop=(tl == 3),
                           inc=((("sqmm", 1),) if (tl == 3 and n == 1) else ()))
            ctx.wait("vector", "sqmm")
            for n in range(2):
                ctx.op("vector", "tensor_copy", out=s_stat[0:1, n * 512:(n + 1) * 512],
                       in_=ctx.psum[4 + n][0:1, 0:512], inc=(("stev", 1),))
            ctx.guard_bank(4, "stev")
            ctx.guard_bank(5, "stev")
            ctx.wait("tensor", "stev")
            ctx.use_bank(4)
            ctx.use_bank(5)
            for n in range(2):
                ctx.op("tensor", "matmul", ctx.psum[4 + n][:, 0:512],
                       ctx.sb["ones_row"][0:1, 0:128],
                       s_stat[0:1, n * 512:(n + 1) * 512], start=True, stop=True,
                       inc=((("bcmm", 1),) if n == 1 else ()))
            ctx.wait("scalar", "bcmm")
            for n in range(2):
                ctx.op("scalar", "activation", out=s_sq[:, n * 512:(n + 1) * 512],
                       in_=ctx.psum[4 + n][:, 0:512], func=AF.Ln,
                       bias=ctx.sb["s_eps6"][:, 0:1], scale=1.0 / 512.0,
                       inc=(("vve", 1),))
            ctx.guard_bank(4, "vve")
            ctx.guard_bank(5, "vve")
            ctx.op("scalar", "activation", out=s_rstd[:, 0:1024], in_=s_sq[:, 0:1024],
                   func=AF.Exp, bias=0.0, scale=-0.5, inc=(("rse", 1),))
            for tl in range(4):
                ob = oh + 3 * 4 + tl
                seq = ob + 32 * chunk
                ws_g.wait_ob(seq)
                for n in range(2):
                    bank = (2 * tl + n) % 4
                    ctx.use_bank(bank)
                    mm_accum(ctx, bank, ws_g, seq, xpft,
                             slice(n * 512, (n + 1) * 512),
                             extra_inc=(("mmg", 1),))
                    ctx.wait("scalar", "mmg")
                    ctx.op("scalar", "activation",
                           out=s_gd[:, tl * 1024 + n * 512: tl * 1024 + (n + 1) * 512],
                           in_=ctx.psum[bank][:, 0:512], func=AF.Sigmoid,
                           bias=ctx.sb["s_bgate"][:, ob:ob + 1], scale=1.0,
                           inc=(("evg", 1),))
                    ctx.guard_bank(bank, "evg")
                if seq + 2 < 64:
                    ws_g.load((ob + 2) % 32, seq=seq + 2,
                              wait=("evg", ctx.count("evg")))
            ctx.wait("vector", "rse")
            ctx.wait("vector", "evg")
            for tl in range(4):
                sl = slice(tl * 1024, (tl + 1) * 1024)
                ctx.op("vector", "scalar_tensor_tensor", out=s_ga[:, sl],
                       in0=s_gc[:, sl], scalar=ctx.sb["s_rms"][:, tl:tl + 1],
                       in1=s_rstd[:, 0:1024], op0=ALU.mult, op1=ALU.mult,
                       inc=(("cn1", 1),))
            ctx.wait("scalar", "cn1")
            for tl in range(4):
                sl = slice(tl * 1024, (tl + 1) * 1024)
                ctx.op("scalar", "activation", out=s_ga[:, sl], in_=s_ga[:, sl],
                       func=AF.Tanh, inc=(("cn2", 1),))
            ctx.wait("vector", "cn2")
            ctx.wait("vector", "a2w")
            hstb = s_hst[:].bitcast(BF16)
            for tl in range(4):
                sl = slice(tl * 1024, (tl + 1) * 1024)
                ctx.op("vector", "scalar_tensor_tensor",
                       out=hstb[:, tl * 1024:(tl + 1) * 1024],
                       in0=s_gd[:, sl], scalar=ctx.sb["s_meo"][:, 0:1],
                       in1=s_ga[:, sl], op0=ALU.mult, op1=ALU.mult)
                ctx.op("vector", "scalar_tensor_tensor",
                       out=hstb[:, (4 + tl) * 1024:(5 + tl) * 1024],
                       in0=s_gd[:, sl], scalar=ctx.sb["s_meo"][:, 1:2],
                       in1=s_ga[:, sl], op0=ALU.mult, op1=ALU.mult,
                       inc=(("hok", 1),))
            ctx.wait("gpsimd", "hok")
            for par in range(2):
                dst = ap["a2_i"].rearrange("(g par hh l p) t -> g par hh p l t",
                                           par=2, hh=2, l=4, p=128)[chunk, par, head]
                ctx.dma("gpsimd", dst,
                        hstb[:, par * 4096:(par + 1) * 4096].rearrange("p (l t) -> p l t", l=4),
                        "a2w")

    ctx.wait("gpsimd", "a2w")
    ctx.op("gpsimd", "collective_compute", "ReduceScatter", ALU.add,
           replica_groups=PAIRS, ins=[ap["a2_i"].opt()], outs=[ap["a2_o"].opt()],
           inc=(("ccs", 1),))

    ctx.wait("sync", "ccs")
    s_hf = pbA
    ctx.dma("sync", s_hf[:].rearrange("p (k t) -> p k t", k=16),
            ap["a2_o"].rearrange("(k p) t -> p k t", p=128), "hfld")
    hf_t = ctx.count("hfld")
    ws_o = WStream(ctx, "w_out", ap["w_outT"], 16, 16, buf=ctx.sb["ws_w_in"])
    ws_o.load(0)
    ws_o.load(1)
    hft = lambda kc: s_hf[:, kc * 1024:(kc + 1) * 1024]
    ctx.wait("tensor", "hfld", hf_t)
    sxt = sx[:].rearrange("p (i t) -> p i t", i=16)
    for ob in range(16):
        ws_o.wait_ob(ob)
        for n in range(2):
            bank = (2 * ob + n) % 4
            ctx.use_bank(bank)
            mm_accum(ctx, bank, ws_o, ob, hft, slice(n * 512, (n + 1) * 512),
                     extra_inc=(("mmo", 1),))
            ctx.wait("vector", "mmo")
            ctx.op("vector", "tensor_add", out=sxt[:, ob][:, n * 512:(n + 1) * 512],
                   in0=sxt[:, ob][:, n * 512:(n + 1) * 512],
                   in1=ctx.psum[bank][:, 0:512], inc=(("reso", 1),))
            ctx.guard_bank(bank, "reso")
        if ob + 2 < 16:
            ws_o.load(ob + 2, wait=("reso", ctx.count("reso")))
    return "reso", ctx.count("reso")



def alloc_common(ctx):
    ctx.alloc_sb("sx", [128, 16 * 1024], F32)
    ctx.alloc_sb("pbA", [128, 16 * 1024], BF16)
    ctx.alloc_sb("s_sq", [128, 2 * 1024], F32)
    ctx.alloc_sb("s_stat", [1, 2 * 1024], F32)
    ctx.alloc_sb("s_m", [128, 1024], F32)
    ctx.alloc_sb("s_rstd", [128, 1024], F32)
    ctx.alloc_sb("s_stage", [128, 2 * 1024], BF16)
    ctx.alloc_sb("ones_col", [128, 1], F32)
    ctx.alloc_sb("ones_col16", [128, 1], BF16)
    ctx.alloc_sb("ones_row", [1, 128], F32)
    ctx.alloc_sb("ones_row16", [1, 128], BF16)
    ctx.alloc_psum_banks(8)
    ctx.op("vector", "memset", ctx.sb["ones_col"][:], 1.0)
    ctx.op("vector", "memset", ctx.sb["ones_col16"][:], 1.0)
    ctx.op("vector", "memset", ctx.sb["ones_row"][:], 1.0)
    ctx.op("vector", "memset", ctx.sb["ones_row16"][:], 1.0)
    ctx.alloc_sb("s_eps5", [128, 1], F32)
    ctx.alloc_sb("s_eps6", [128, 1], F32)
    ctx.op("vector", "memset", ctx.sb["s_eps5"][:], LN_EPS)
    ctx.op("vector", "memset", ctx.sb["s_eps6"][:], SC_EPS, inc=(("vinit", 1),))


def alloc_l0attn(ctx):
    ctx.alloc_sb("s_bgate", [128, 32], F32)
    ctx.alloc_sb("s_rms", [128, 4], F32)
    ctx.alloc_sb("s_ga", [128, 4 * 1024], F32)
    ctx.alloc_sb("s_gb", [128, 4 * 1024], F32)
    ctx.alloc_sb("s_gc", [128, 4 * 1024], F32)
    ctx.alloc_sb("s_carry", [128, 8], F32)
    ctx.alloc_sb("s_meo", [128, 2], F32)
    ctx.alloc_sb("s_hstage", [128, 4 * 1024], F32)



def build_l0attn_program():
    with allow_ici_all_to_all():
        return _build_l0attn_program()


def _build_l0attn_program():
    nc = bass.Bass("TRN2", target_bir_lowering=False, debug=False)
    ap = {}
    ap["x_in"] = nc.dram_tensor("x_in", [2048, 1024], F32, kind="ExternalInput").ap()
    ap["w_inT"] = nc.dram_tensor("w_inT", [16, 16, 128, 128], F32, kind="ExternalInput").ap()
    ap["w_gateT"] = nc.dram_tensor("w_gateT", [32, 16, 128, 128], F32, kind="ExternalInput").ap()
    ap["b_gate"] = nc.dram_tensor("b_gate", [128, 32], F32, kind="ExternalInput").ap()
    ap["rms_w"] = nc.dram_tensor("rms_w", [128, 4], F32, kind="ExternalInput").ap()
    ap["m_eo"] = nc.dram_tensor("m_eo", [128, 2], F32, kind="ExternalInput").ap()
    ap["w_outT"] = nc.dram_tensor("w_outT", [16, 16, 128, 128], F32, kind="ExternalInput").ap()
    ap["x_out"] = nc.dram_tensor("x_out", [2048, 1024], F32, kind="ExternalOutput").ap()
    ap["ag_i"] = nc.dram_tensor("ag_i", [2048, 1024], BF16).ap()
    ap["ag_o"] = nc.dram_tensor("ag_o", [4096, 1024], BF16).ap()
    ap["a2_i"] = nc.dram_tensor("a2_i", [4096, 1024], BF16).ap()
    ap["a2_o"] = nc.dram_tensor("a2_o", [2048, 1024], BF16).ap()

    es = contextlib.ExitStack()
    with es:
        ctx = Ctx(nc, es)
        alloc_common(ctx)
        alloc_l0attn(ctx)
        ctx.dma("sync", ctx.sb["sx"][:].rearrange("p (i t) -> p i t", i=16),
                ap["x_in"].rearrange("(i p) t -> p i t", p=128), "xld")
        ctx.wait("vector", "xld")
        ctx.wait("vector", "vinit")
        s, cnt = ph_l0attn(ctx, ap)
        ctx.wait("gpsimd", s, cnt)
        ctx.dma("gpsimd", ap["x_out"].rearrange("(i p) t -> p i t", p=128),
                ctx.sb["sx"][:].rearrange("p (i t) -> p i t", i=16), "xst")
        ctx.wait("gpsimd", "xst")
        block = es.enter_context(nc.Block())
        ctx.finalize(block)
    return nc


_PH1_KEYS = ["w_inT", "w_gateT", "b_gate", "rms_w", "w_outT", "m_eo"]
_PH3_KEYS = ["wkT", "wvT", "wrT", "woT", "bk", "bv", "br", "bo", "tmk", "tmv",
             "tmr", "td", "tf", "m_eo", "zcol"]
_PH4_KEYS = ["gu_pkT", "gu_scT", "dn_pkT", "dn_scT"]


def kernel(**inputs):
    pcs = [prep_core(inputs, c) for c in range(8)]
    xs = [p["x_c"] for p in pcs]

    nc1 = build_l0attn_program()
    im = [dict({k: pcs[c][k] for k in _PH1_KEYS}, x_in=xs[c]) for c in range(8)]
    res = run_bass_kernel_spmd(nc1, im, core_ids=list(range(8)))
    xs = [res.results[c]["x_out"] for c in range(8)]

    nc2 = build_l0ffn_program()
    im = [dict(x_in=xs[c], w1T=pcs[c]["w1T"], w3T=pcs[c]["w3T"], w2T=pcs[c]["w2T"],
               h128=pcs[c]["h128"], h64x2=pcs[c]["h64x2"]) for c in range(8)]
    res = run_bass_kernel_spmd(nc2, im, core_ids=list(range(8)))
    xs = [res.results[c]["x_out"] for c in range(8)]

    nc3 = build_l1attn_program()
    im = [dict({k: pcs[c][k] for k in _PH3_KEYS}, x_in=xs[c]) for c in range(8)]
    res = run_bass_kernel_spmd(nc3, im, core_ids=list(range(8)))
    xs = [res.results[c]["x_out"] for c in range(8)]

    nc4 = build_l1ffn_program()
    im = [dict({k: pcs[c][k] for k in _PH4_KEYS}, x_in=xs[c]) for c in range(8)]
    res = run_bass_kernel_spmd(nc4, im, core_ids=list(range(8)))
    xs = [res.results[c]["x_out"] for c in range(8)]

    out = np.empty((4, 2048, 2048), np.float32)
    for c in range(8):
        b, s = c // 2, c % 2
        out[b, s * 1024:(s + 1) * 1024] = unpermute_out(xs[c])
    return out



def ph_l0ffn(ctx, ap):
    sx = ctx.sb["sx"]
    pbA = ctx.sb["pbA"]
    pool = sx[:].bitcast(BF16)
    s_had = ctx.sb["s_had"]
    s_sg = ctx.sb["s_sg"]
    s_zst = ctx.sb["s_zst"]
    s_zt = ctx.sb["s_zt"]
    s_xt = ctx.sb["s_xt"]
    s_xo = ctx.sb["s_xo"]

    ctx.dma("sync", s_had[:, 0:128], ap["h128"], "cldf")
    ctx.dma("sync", s_had[:, 128:192], ap["h64x2"], "cldf")

    lnsem, lncnt = emit_layernorm(ctx, sx, pbA, "ln2")

    ws1 = WStream(ctx, "w1", ap["w1T"], 64, 16)
    ws3 = WStream(ctx, "w3", ap["w3T"], 64, 16)
    ws2 = WStream(ctx, "w2", ap["w2T"], 32, 32)
    xlnt = lambda kc: pbA[:, kc * 1024:(kc + 1) * 1024]
    ctx.wait("tensor", lnsem, lncnt)
    ctx.wait("tensor", "cldf", 32)
    SCL = float(1.0 / np.sqrt(8192.0))

    ws1.load(0); ws3.load(0); ws1.load(1); ws3.load(1)
    for n in range(2):
        nsl = slice(n * 512, (n + 1) * 512)
        for ob in range(64):
            seq = n * 64 + ob
            ws1.wait_ob(seq); ws3.wait_ob(seq)
            bg = (2 * ob) % 4
            bu = (2 * ob + 1) % 4
            ctx.use_bank(bg)
            mm_accum(ctx, bg, ws1, seq, xlnt, nsl, extra_inc=(("mmf", 1),))
            ctx.use_bank(bu)
            mm_accum(ctx, bu, ws3, seq, xlnt, nsl, extra_inc=(("mmf", 1),))
            ctx.wait("scalar", "mmf")
            if ctx.count("hidr") >= 1:
                ctx.wait("scalar", "hidr", ctx.count("hidr") - 1)
            ctx.op("scalar", "activation", out=s_sg[:, (ob % 2) * 512:(ob % 2 + 1) * 512],
                   in_=ctx.psum[bg][:, 0:512], func=AF.Silu, inc=(("sil", 1),))
            ctx.guard_bank(bg, "sil")
            ctx.wait("vector", "sil")
            ctx.op("vector", "tensor_mul",
                   out=pool[:, ob * 512:(ob + 1) * 512],
                   in0=s_sg[:, (ob % 2) * 512:(ob % 2 + 1) * 512],
                   in1=ctx.psum[bu][:, 0:512], inc=(("hidr", 1),))
            ctx.guard_bank(bu, "hidr")
            bz = 4 + ob % 2
            ctx.use_bank(bz)
            ctx.wait("tensor", "hidr")
            ctx.op("tensor", "matmul", ctx.psum[bz][:, 0:512], s_had[:, 0:128],
                   pool[:, ob * 512:(ob + 1) * 512], start=True, stop=True,
                   inc=(("mmza", 1),))
            ctx.wait("vector", "mmza")
            if seq >= 2:
                ctx.wait("vector", "vw", 16 * (seq - 1))
            ctx.op("vector", "tensor_scalar",
                   out=s_zst[:, (ob % 2) * 512:(ob % 2 + 1) * 512],
                   in0=ctx.psum[bz][:, 0:512], scalar1=SCL, scalar2=None,
                   op0=ALU.mult, inc=(("zae", 1),))
            ctx.guard_bank(bz, "zae")
            ctx.wait("sync", "zae")
            ctx.dma("sync", ap["V"].rearrange("(a p) t -> p a t", p=128)[:, ob][:, nsl],
                    s_zst[:, (ob % 2) * 512:(ob % 2 + 1) * 512], "vw")
            if seq + 2 < 128:
                ws1.load((ob + 2) % 64, seq=seq + 2, wait=("sil", ctx.count("sil")))
                ws3.load((ob + 2) % 64, seq=seq + 2, wait=("hidr", ctx.count("hidr")))
        vw_t = ctx.count("vw")
        if n == 0:
            ws2.load(0, seq=0); ws2.load(1, seq=1)
        for bg8 in range(8):
            ctx.wait("sync", "vw", vw_t)
            if ctx.count("mmzb") >= 8:
                ctx.wait("sync", "mmzb", ctx.count("mmzb") - 8)
            slot = bg8 % 2
            zz = ap["V"].rearrange("(a s2 g8 b8) t -> s2 a g8 b8 t", a=64, s2=2, g8=8)
            dst = s_zt[:].rearrange("p (sl b t) -> p sl b t", sl=2, b=8)
            ctx.dma("sync", dst[0:64, slot], zz[0, :, bg8][:, :, nsl], "ztl")
            ctx.dma("sync", dst[64:128, slot], zz[1, :, bg8][:, :, nsl], "ztl")
            ctx.wait("tensor", "ztl")
            for b in range(8):
                beta = bg8 * 8 + b
                bb = 6 + (beta % 2)
                ctx.use_bank(bb)
                ctx.op("tensor", "matmul", ctx.psum[bb][0:64, 0:512],
                       s_had[0:64, 128:192],
                       s_zt[:].rearrange("p (sl b t) -> p sl b t", sl=2, b=8)[0:64, slot, b],
                       start=True, stop=True)
                ctx.op("tensor", "matmul", ctx.psum[bb][64:128, 0:512],
                       s_had[64:128, 128:192],
                       s_zt[:].rearrange("p (sl b t) -> p sl b t", sl=2, b=8)[64:128, slot, b],
                       start=True, stop=True, tile_position=(64, 64),
                       inc=(("mmzb", 1),))
                ctx.wait("vector", "mmzb")
                ctx.op("vector", "tensor_copy", out=pool[:, beta * 512:(beta + 1) * 512],
                       in_=ctx.psum[bb][:, 0:512], inc=(("zbe", 1),))
                ctx.guard_bank(bb, "zbe")
        ctx.wait("tensor", "zbe")
        for ob in range(2):
            ctx.dma("sync", s_xt[:].rearrange("p (o t) -> p o t", o=2)[:, ob % 2][:, nsl],
                    ap["x_dram"].rearrange("(o p) t -> o p t", p=128)[ob][:, nsl], "xtl")
        for ob in range(16):
            bank = ob % 4
            ctx.use_bank(bank)
            for half in range(2):
                seq = n * 32 + ob * 2 + half
                ws2.wait_ob(seq)
                for kc in range(32):
                    beta = half * 32 + kc
                    ctx.op("tensor", "matmul", ctx.psum[bank][:, 0:512],
                           ws2.lhsT(seq, kc), pool[:, beta * 512:(beta + 1) * 512],
                           start=(half == 0 and kc == 0), stop=(half == 1 and kc == 31),
                           inc=((("mmw2", 1),) if (half == 1 and kc == 31) else ()))
            if n * 32 + (ob + 1) * 2 + 1 < 64:
                ws2.load(((ob + 1) * 2) % 32, seq=n * 32 + (ob + 1) * 2,
                         wait=("mmw2", ctx.count("mmw2")))
                ws2.load(((ob + 1) * 2 + 1) % 32, seq=n * 32 + (ob + 1) * 2 + 1)
            ctx.wait("vector", "mmw2")
            ctx.wait("vector", "xtl", 16 * (n * 16 + ob + 1))
            if n * 16 + ob >= 2:
                ctx.wait("vector", "xow", 16 * (n * 16 + ob - 1))
            ctx.op("vector", "tensor_add",
                   out=s_xo[:, (ob % 2) * 1024 + n * 512:(ob % 2) * 1024 + (n + 1) * 512],
                   in0=s_xt[:, (ob % 2) * 1024 + n * 512:(ob % 2) * 1024 + (n + 1) * 512],
                   in1=ctx.psum[bank][:, 0:512], inc=(("w2ev", 1),))
            ctx.guard_bank(bank, "w2ev")
            ctx.wait("gpsimd", "w2ev")
            ctx.dma("gpsimd",
                    ap["x_out"].rearrange("(o p) t -> o p t", p=128)[ob][:, nsl],
                    s_xo[:, (ob % 2) * 1024 + n * 512:(ob % 2) * 1024 + (n + 1) * 512],
                    "xow")
            if ob + 2 < 16:
                ctx.wait("sync", "w2ev")
                ctx.dma("sync", s_xt[:].rearrange("p (o t) -> p o t", o=2)[:, ob % 2][:, nsl],
                        ap["x_dram"].rearrange("(o p) t -> o p t", p=128)[ob + 2][:, nsl], "xtl")
    return "xow", ctx.count("xow")


def alloc_l0ffn(ctx):
    ctx.alloc_sb("s_had", [128, 192], BF16)
    ctx.alloc_sb("s_sg", [128, 2 * 512], F32)
    ctx.alloc_sb("s_zst", [128, 2 * 512], BF16)
    ctx.alloc_sb("s_zt", [128, 2 * 8 * 512], BF16)
    ctx.alloc_sb("s_xt", [128, 2 * 1024], F32)
    ctx.alloc_sb("s_xo", [128, 2 * 1024], F32)


def build_l0ffn_program():
    with allow_ici_all_to_all():
        return _build_l0ffn_program()


def _build_l0ffn_program():
    nc = bass.Bass("TRN2", target_bir_lowering=False, debug=False)
    ap = {}
    ap["x_in"] = nc.dram_tensor("x_in", [2048, 1024], F32, kind="ExternalInput").ap()
    ap["w1T"] = nc.dram_tensor("w1T", [64, 16, 128, 128], F32, kind="ExternalInput").ap()
    ap["w3T"] = nc.dram_tensor("w3T", [64, 16, 128, 128], F32, kind="ExternalInput").ap()
    ap["w2T"] = nc.dram_tensor("w2T", [32, 32, 128, 128], F32, kind="ExternalInput").ap()
    ap["h128"] = nc.dram_tensor("h128", [128, 128], BF16, kind="ExternalInput").ap()
    ap["h64x2"] = nc.dram_tensor("h64x2", [128, 64], BF16, kind="ExternalInput").ap()
    ap["x_out"] = nc.dram_tensor("x_out", [2048, 1024], F32, kind="ExternalOutput").ap()
    ap["V"] = nc.dram_tensor("V", [8192, 1024], BF16).ap()
    ap["x_dram"] = ap["x_in"]

    es = contextlib.ExitStack()
    with es:
        ctx = Ctx(nc, es)
        alloc_common(ctx)
        alloc_l0ffn(ctx)
        ctx.dma("sync", ctx.sb["sx"][:].rearrange("p (i t) -> p i t", i=16),
                ap["x_in"].rearrange("(i p) t -> p i t", p=128), "xld")
        ctx.wait("vector", "xld")
        ctx.wait("vector", "vinit")
        s, cnt = ph_l0ffn(ctx, ap)
        ctx.wait("gpsimd", s, cnt)
        block = es.enter_context(nc.Block())
        ctx.finalize(block)
    return nc



def ph_l1attn(ctx, ap):
    sx = ctx.sb["sx"]
    pbA = ctx.sb["pbA"]
    pbAb = pbA[:]
    sxb = sx[:].bitcast(BF16)

    lnsem, lncnt = emit_layernorm(ctx, sx, pbA, "ln3")
    ctx.wait("gpsimd", lnsem, lncnt)
    ctx.dma("gpsimd", ap["ag3_i"].rearrange("(i p) t -> p i t", p=128),
            pbA[:].rearrange("p (i t) -> p i t", i=16), "ag3w")
    ctx.wait("gpsimd", "ag3w")
    ctx.op("gpsimd", "collective_compute", "AllGather", ALU.bypass,
           replica_groups=PAIRS, ins=[ap["ag3_i"].opt()], outs=[ap["ag3_o"].opt()],
           inc=(("ccs3", 1),))

    ctx.dma("sync", ctx.sb["s_bkv"][:, 0:8], ap["bk"], "cld3")
    ctx.dma("sync", ctx.sb["s_bkv"][:, 8:16], ap["bv"], "cld3")
    ctx.dma("sync", ctx.sb["s_bkv"][:, 16:24], ap["br"], "cld3")
    ctx.dma("sync", ctx.sb["s_tm"][:, 0:16], ap["tmk"], "cld3")
    ctx.dma("sync", ctx.sb["s_tm"][:, 16:32], ap["tmv"], "cld3")
    ctx.dma("sync", ctx.sb["s_tm"][:, 32:48], ap["tmr"], "cld3")
    ctx.dma("sync", ctx.sb["s_tdtf"][:, 0:8], ap["td"], "cld3")
    ctx.dma("sync", ctx.sb["s_tdtf"][:, 8:16], ap["tf"], "cld3")
    ctx.dma("sync", ctx.sb["s_bo"][:], ap["bo"], "cld3")
    ctx.dma("sync", ctx.sb["s_meo"][:], ap["m_eo"], "cld3")
    ctx.wait("gpsimd", lnsem, lncnt)
    for ob in range(8):
        ctx.dma("gpsimd",
                sxb[:, ob * 2048:(ob + 1) * 2048].rearrange("p (k m) -> p k m", k=16),
                ap["wkT"][ob].rearrange("k p m -> p k m"), "wres")
        ctx.dma("gpsimd",
                sxb[:, 16384 + ob * 2048:16384 + (ob + 1) * 2048].rearrange("p (k m) -> p k m", k=16),
                ap["wvT"][ob].rearrange("k p m -> p k m"), "wres")
        ctx.dma("gpsimd",
                ctx.sb["s_wr"][:, ob * 2048:(ob + 1) * 2048].rearrange("p (k m) -> p k m", k=16),
                ap["wrT"][ob].rearrange("k p m -> p k m"), "wres")

    Tc1 = 256
    xfw = 257
    bxk_off = 16 * xfw
    bxv_off = bxk_off + 16 * 256
    s_dt = ctx.sb["s_stage"]
    s_kst = ctx.sb["s_kst"]
    s_vst = ctx.sb["s_vst"]
    bxr = ctx.sb["s_sq"][:].bitcast(BF16)
    ctx.wait("tensor", "cld3", ctx.count("cld3"))
    ctx.wait("scalar", "cld3", ctx.count("cld3"))
    ctx.wait("vector", "cld3", ctx.count("cld3"))
    ctx.wait("tensor", "wres", 16 * 24)
    ccs3_t = ctx.count("ccs3")
    for c in range(8):
        t0 = c * Tc1
        blk = t0 // 1024
        tloc = t0 % 1024
        ctx.wait("sync", "ccs3", ccs3_t)
        if c > 0:
            ctx.wait("sync", "mixd")
        agv = ap["ag3_o"].rearrange("(b k p) t -> b p k t", b=2, p=128)
        ctx.dma("sync", pbA[:, 0:16 * xfw].rearrange("p (k w) -> p k w", k=16, w=xfw)[:, :, 1:257],
                agv[blk][:, :, tloc:tloc + 256], "xfl")
        if c == 0:
            ctx.dma("sync", pbA[:, 0:16 * xfw].rearrange("p (k w) -> p k w", k=16, w=xfw)[:, :, 0:1],
                    ap["zcol"].unsqueeze(2), "xfl",
                    allow_slow_non_contiguous=True)
        else:
            tp = (t0 - 1) % 1024
            ctx.dma("sync", pbA[:, 0:16 * xfw].rearrange("p (k w) -> p k w", k=16, w=xfw)[:, :, 0:1],
                    agv[(t0 - 1) // 1024][:, :, tp:tp + 1], "xfl",
                    allow_slow_non_contiguous=True)
        xfl_t = ctx.count("xfl")
        ctx.wait("vector", "xfl", xfl_t)
        if ctx.count("mmk") > 0:
            ctx.wait("vector", "mmk")
        xfv = pbA[:, 0:16 * xfw].rearrange("p (k w) -> p k w", k=16, w=xfw)
        for kc in range(16):
            ctx.op("vector", "tensor_sub", out=s_dt[:, 0:256],
                   in0=xfv[:, kc, 1:257], in1=xfv[:, kc, 0:256])
            ctx.op("vector", "scalar_tensor_tensor",
                   out=pbAb[:, bxk_off + kc * 256: bxk_off + (kc + 1) * 256],
                   in0=s_dt[:, 0:256], scalar=ctx.sb["s_tm"][:, kc:kc + 1],
                   in1=xfv[:, kc, 0:256], op0=ALU.mult, op1=ALU.add)
            ctx.op("vector", "scalar_tensor_tensor",
                   out=pbAb[:, bxv_off + kc * 256: bxv_off + (kc + 1) * 256],
                   in0=s_dt[:, 0:256], scalar=ctx.sb["s_tm"][:, 16 + kc:17 + kc],
                   in1=xfv[:, kc, 0:256], op0=ALU.mult, op1=ALU.add)
            ctx.op("vector", "scalar_tensor_tensor",
                   out=bxr[:, kc * 256:(kc + 1) * 256],
                   in0=s_dt[:, 0:256], scalar=ctx.sb["s_tm"][:, 32 + kc:33 + kc],
                   in1=xfv[:, kc, 0:256], op0=ALU.mult, op1=ALU.add,
                   inc=(("mixd", 1),))
        mix_t = ctx.count("mixd")
        ctx.wait("tensor", "mixd", mix_t)
        for pi in range(3):
            if c > 0:
                ctx.wait("scalar", "kvw", 16 * (3 * (c - 1) + pi + 1))
            for ob in range(8):
                bank = ob
                ctx.use_bank(bank)
                for kc in range(16):
                    if pi == 0:
                        lhsT = sxb[:, (ob * 16 + kc) * 128:(ob * 16 + kc) * 128 + 128]
                        rhs = pbAb[:, bxk_off + kc * 256: bxk_off + (kc + 1) * 256]
                    elif pi == 1:
                        lhsT = sxb[:, 16384 + (ob * 16 + kc) * 128:16384 + (ob * 16 + kc) * 128 + 128]
                        rhs = pbAb[:, bxv_off + kc * 256: bxv_off + (kc + 1) * 256]
                    else:
                        lhsT = ctx.sb["s_wr"][:, (ob * 16 + kc) * 128:(ob * 16 + kc) * 128 + 128]
                        rhs = bxr[:, kc * 256:(kc + 1) * 256]
                    ctx.op("tensor", "matmul", ctx.psum[bank][:, 0:256],
                           lhsT, rhs, start=(kc == 0), stop=(kc == 15),
                           inc=((("mmk", 1),) if kc == 15 else ()))
                ctx.wait("scalar", "mmk")
                if pi == 0:
                    ctx.op("scalar", "activation", out=s_kst[:, ob * 256:(ob + 1) * 256],
                           in_=ctx.psum[bank][:, 0:256], func=AF.Identity,
                           bias=ctx.sb["s_bkv"][:, ob:ob + 1], scale=1.0,
                           inc=(("evk", 1),))
                elif pi == 1:
                    ctx.op("scalar", "activation",
                           out=s_vst[:].bitcast(BF16)[:, ob * 256:(ob + 1) * 256],
                           in_=ctx.psum[bank][:, 0:256], func=AF.Identity,
                           bias=ctx.sb["s_bkv"][:, 8 + ob:9 + ob], scale=1.0,
                           inc=(("evk", 1),))
                else:
                    ctx.op("scalar", "activation",
                           out=s_vst[:].bitcast(BF16)[:, 2048 + ob * 256:2048 + (ob + 1) * 256],
                           in_=ctx.psum[bank][:, 0:256], func=AF.Sigmoid,
                           bias=ctx.sb["s_bkv"][:, 16 + ob:17 + ob], scale=1.0,
                           inc=(("evk", 1),))
                ctx.guard_bank(bank, "evk")
            ctx.wait("gpsimd", "evk")
            if pi == 0:
                ctx.dma("gpsimd", ap["kbuf"].rearrange("(o p) t -> p o t", p=128)[:, :, t0:t0 + 256],
                        s_kst[:].rearrange("p (o t) -> p o t", o=8), "kvw")
            elif pi == 1:
                ctx.dma("gpsimd", ap["vbuf"].rearrange("(o p) t -> p o t", p=128)[:, :, t0:t0 + 256],
                        s_vst[:].bitcast(BF16)[:, 0:2048].rearrange("p (o t) -> p o t", o=8), "kvw")
            else:
                ctx.dma("gpsimd", ap["srbuf"].rearrange("(o p) t -> p o t", p=128)[:, :, t0:t0 + 256],
                        s_vst[:].bitcast(BF16)[:, 2048:4096].rearrange("p (o t) -> p o t", o=8), "kvw")

    sxf = sx[:]
    s_k2 = sxf[:, 0:4096]
    s_q = sxf[:, 4096:8192]
    s_N = sxf[:, 8192:12288]
    s_v2 = sxf[:, 12288:14336].bitcast(BF16)
    s_sr2 = sxf[:, 14336:16384].bitcast(BF16)
    pbf = pbA[:].bitcast(F32)
    s_aa = pbf[:, 0:4096]
    s_bb = pbf[:, 4096:8192]
    s_qv = ctx.sb["s_qv"]
    s_m8 = ctx.sb["s_m8"]
    s_cr = ctx.sb["s_cr"]
    s_o5 = ctx.sb["s_o5"]
    ctx.wait("scalar", "kvw", ctx.count("kvw"))
    ctx.op("scalar", "memzero", s_cr[:], inc=(("p2i", 1),))
    ctx.op("scalar", "activation", out=s_m8[:, 40:48], in_=ctx.sb["s_tdtf"][:, 0:8],
           func=AF.Exp)
    ctx.op("scalar", "activation", out=s_m8[:, 32:40], in_=ctx.sb["s_tdtf"][:, 8:16],
           func=AF.Exp)
    ctx.op("scalar", "activation", out=s_m8[:, 40:48], in_=s_m8[:, 40:48],
           func=AF.Exp, scale=-1.0, inc=(("p2c", 1),))
    ctx.wait("vector", "p2c")
    ctx.op("vector", "memset", s_o5[:, 0:512], 1.0, inc=(("p2w", 1),))

    for c in range(4):
        t0 = c * 512
        ctx.wait("sync", "kvw", 16 * 3 * 8)
        if c > 0:
            ctx.wait("sync", "rwm")
        ctx.dma("sync", s_k2.rearrange("p (o t) -> p o t", o=8),
                ap["kbuf"].rearrange("(o p) t -> p o t", p=128)[:, :, t0:t0 + 512], "k2l")
        ctx.dma("sync", s_v2.rearrange("p (o t) -> p o t", o=8),
                ap["vbuf"].rearrange("(o p) t -> p o t", p=128)[:, :, t0:t0 + 512], "k2l")
        ctx.dma("sync", s_sr2.rearrange("p (o t) -> p o t", o=8),
                ap["srbuf"].rearrange("(o p) t -> p o t", p=128)[:, :, t0:t0 + 512], "k2l")
        k2l_t = ctx.count("k2l")
        ctx.wait("vector", "k2l", k2l_t)
        ctx.wait("vector", "p2i")
        ctx.wait("vector", "p2w")
        for ct in range(8):
            ctx.op("vector", "reduce_max", out=s_m8[:, ct:ct + 1],
                   in_=s_k2[:, ct * 512:(ct + 1) * 512], axis=mybir.AxisListType.X)
        ctx.op("vector", "tensor_scalar", out=s_m8[:, 16:24], in0=s_m8[:, 0:8],
               scalar1=-1.0, scalar2=None, op0=ALU.mult, inc=(("mred", 1),))
        if c > 0:
            ctx.op("vector", "tensor_sub", out=s_m8[:, 24:32], in0=s_m8[:, 8:16],
                   in1=s_m8[:, 0:8], inc=(("mred", 1),))
            ctx.wait("scalar", "mred")
            ctx.op("scalar", "activation", out=s_m8[:, 24:32], in_=s_m8[:, 24:32],
                   func=AF.Exp, inc=(("rsce", 1),))
            ctx.wait("scalar", "rsce")
            for ct in range(8):
                ctx.op("scalar", "mul", out=s_cr[:, ct:ct + 1],
                       in_=s_cr[:, ct:ct + 1], mul=s_m8[:, 24 + ct:25 + ct])
                ctx.op("scalar", "mul", out=s_cr[:, 8 + ct:9 + ct],
                       in_=s_cr[:, 8 + ct:9 + ct], mul=s_m8[:, 24 + ct:25 + ct],
                       inc=((("carr", 1),) if ct == 7 else ()))
        else:
            ctx.wait("scalar", "mred")
        ctx.op("scalar", "copy", out=s_m8[:, 8:16], in_=s_m8[:, 0:8], inc=(("mcp", 1),))
        ctx.wait("scalar", "mcp")
        for ct in range(8):
            ctx.op("scalar", "activation", out=s_q[:, ct * 512:(ct + 1) * 512],
                   in_=s_k2[:, ct * 512:(ct + 1) * 512], func=AF.Exp,
                   bias=s_m8[:, 16 + ct:17 + ct], scale=1.0,
                   inc=((("qex", 1),) if ct == 7 else ()))
        ctx.wait("vector", "qex")
        ctx.op("vector", "tensor_mul", out=s_qv[:], in0=s_q[:, 0:4096], in1=s_v2,
               inc=(("qve", 1),))
        ctx.wait("vector", "carr")
        for ct in range(8):
            ctx.op("vector", "tensor_scalar", out=s_o5[:, 512:1024],
                   in0=s_o5[:, 0:512], scalar1=s_m8[:, 40 + ct:41 + ct],
                   scalar2=None, op0=ALU.mult)
            ctx.op("vector", "tensor_tensor_scan",
                   out=s_aa[:, ct * 512:(ct + 1) * 512],
                   data0=s_o5[:, 512:1024],
                   data1=s_qv[:, ct * 512:(ct + 1) * 512],
                   initial=s_cr[:, ct:ct + 1],
                   op0=ALU.mult, op1=ALU.add)
            ctx.op("vector", "tensor_tensor_scan",
                   out=s_bb[:, ct * 512:(ct + 1) * 512],
                   data0=s_o5[:, 512:1024],
                   data1=s_q[:, ct * 512:(ct + 1) * 512],
                   initial=s_cr[:, 8 + ct:9 + ct],
                   op0=ALU.mult, op1=ALU.add, inc=(("scn2", 1),))
        ctx.wait("vector", "scn2")
        for ct in range(8):
            ctx.op("vector", "scalar_tensor_tensor",
                   out=s_N[:, ct * 512:ct * 512 + 1],
                   in0=s_qv[:, ct * 512:ct * 512 + 1], scalar=s_m8[:, 32 + ct:33 + ct],
                   in1=s_cr[:, ct:ct + 1], op0=ALU.mult, op1=ALU.add)
            ctx.op("vector", "scalar_tensor_tensor",
                   out=s_N[:, ct * 512 + 1:(ct + 1) * 512],
                   in0=s_qv[:, ct * 512 + 1:(ct + 1) * 512], scalar=s_m8[:, 32 + ct:33 + ct],
                   in1=s_aa[:, ct * 512: ct * 512 + 511], op0=ALU.mult, op1=ALU.add)
            ctx.op("vector", "scalar_tensor_tensor",
                   out=s_k2[:, ct * 512:ct * 512 + 1],
                   in0=s_q[:, ct * 512:ct * 512 + 1], scalar=s_m8[:, 32 + ct:33 + ct],
                   in1=s_cr[:, 8 + ct:9 + ct], op0=ALU.mult, op1=ALU.add)
            ctx.op("vector", "scalar_tensor_tensor",
                   out=s_k2[:, ct * 512 + 1:(ct + 1) * 512],
                   in0=s_q[:, ct * 512 + 1:(ct + 1) * 512], scalar=s_m8[:, 32 + ct:33 + ct],
                   in1=s_bb[:, ct * 512: ct * 512 + 511], op0=ALU.mult, op1=ALU.add,
                   inc=(("ndq", 1),))
        ctx.wait("scalar", "ndq")
        for ct in range(8):
            ctx.op("scalar", "copy", out=s_cr[:, ct:ct + 1],
                   in_=s_aa[:, ct * 512 + 511: ct * 512 + 512])
            ctx.op("scalar", "copy", out=s_cr[:, 8 + ct:9 + ct],
                   in_=s_bb[:, ct * 512 + 511: ct * 512 + 512],
                   inc=((("carr", 1),) if ct == 7 else ()))
        ctx.op("scalar", "activation", out=s_q[:, 0:4096], in_=s_k2[:, 0:4096],
               func=AF.Ln)
        ctx.op("scalar", "activation", out=s_k2[:, 0:4096], in_=s_q[:, 0:4096],
               func=AF.Exp, scale=-1.0, inc=(("rcd", 1),))
        ctx.wait("vector", "rcd")
        ctx.op("vector", "tensor_mul", out=s_q[:, 0:4096], in0=s_N[:, 0:4096],
               in1=s_k2[:, 0:4096])
        ctx.op("vector", "scalar_tensor_tensor", out=s_qv[:],
               in0=s_q[:, 0:4096], scalar=ctx.sb["s_meo"][:, 0:1], in1=s_sr2,
               op0=ALU.mult, op1=ALU.mult)
        ctx.op("vector", "scalar_tensor_tensor", out=s_v2,
               in0=s_q[:, 0:4096], scalar=ctx.sb["s_meo"][:, 1:2], in1=s_sr2,
               op0=ALU.mult, op1=ALU.mult, inc=(("rwe", 1),))
        ctx.wait("gpsimd", "rwe")
        blk = t0 // 1024
        tloc = t0 % 1024
        rwv = ap["rw_i"].rearrange("(b par o p) t -> b par p o t", b=2, par=2, o=8)
        ctx.dma("gpsimd", rwv[blk, 0][:, :, tloc:tloc + 512],
                s_qv[:].rearrange("p (o t) -> p o t", o=8), "rwm")
        ctx.dma("gpsimd", rwv[blk, 1][:, :, tloc:tloc + 512],
                s_v2.rearrange("p (o t) -> p o t", o=8), "rwm")

    ctx.wait("gpsimd", "rwm")
    ctx.op("gpsimd", "collective_compute", "ReduceScatter", ALU.add,
           replica_groups=PAIRS, ins=[ap["rw_i"].opt()], outs=[ap["rw_o"].opt()],
           inc=(("ccs3", 1),))
    ctx.wait("sync", "ccs3", ctx.count("ccs3"))
    ctx.wait("sync", "ndq")
    ctx.dma("sync", pbA[:].rearrange("p (k t) -> p k t", k=16),
            ap["rw_o"].rearrange("(k p) t -> p k t", p=128), "rwfl")
    rwf_t = ctx.count("rwfl")
    ws_o = WStream(ctx, "wo", ap["woT"], 16, 16, buf=ctx.sb["s_wr"])
    ws_o.load(0)
    ws_o.load(1)
    rwt = lambda kc: pbA[:, kc * 1024:(kc + 1) * 1024]
    s_xt = ctx.sb["s_xt"]
    s_xo = ctx.sb["s_xo"]
    ctx.wait("tensor", "rwfl", rwf_t)
    for ob in range(2):
        ctx.dma("sync", s_xt[:].rearrange("p (o t) -> p o t", o=2)[:, ob % 2],
                ap["x_dram"].rearrange("(o p) t -> o p t", p=128)[ob], "xt3")
    for ob in range(16):
        ws_o.wait_ob(ob)
        for n in range(2):
            bank = (2 * ob + n) % 4
            ctx.use_bank(bank)
            mm_accum(ctx, bank, ws_o, ob, rwt, slice(n * 512, (n + 1) * 512),
                     extra_inc=(("mmo3", 1),))
            ctx.wait("vector", "mmo3")
            ctx.wait("vector", "xt3", 16 * (ob + 1))
            if ob >= 2 and n == 0:
                ctx.wait("vector", "xo3", 16 * (ob - 1))
            ctx.op("vector", "scalar_tensor_tensor",
                   out=s_xo[:, (ob % 2) * 1024 + n * 512:(ob % 2) * 1024 + (n + 1) * 512],
                   in0=ctx.psum[bank][:, 0:512], scalar=ctx.sb["s_bo"][:, ob:ob + 1],
                   in1=s_xt[:, (ob % 2) * 1024 + n * 512:(ob % 2) * 1024 + (n + 1) * 512],
                   op0=ALU.add, op1=ALU.add, inc=(("wo3e", 1),))
            ctx.guard_bank(bank, "wo3e")
        ctx.wait("gpsimd", "wo3e")
        ctx.dma("gpsimd", ap["x_out"].rearrange("(o p) t -> o p t", p=128)[ob],
                s_xo[:].rearrange("p (o t) -> p o t", o=2)[:, ob % 2], "xo3")
        if ob + 2 < 16:
            ctx.wait("sync", "wo3e")
            ctx.dma("sync", s_xt[:].rearrange("p (o t) -> p o t", o=2)[:, ob % 2],
                    ap["x_dram"].rearrange("(o p) t -> o p t", p=128)[ob + 2], "xt3")
            ws_o.load(ob + 2, seq=ob + 2, wait=("wo3e", ctx.count("wo3e")))
    return "xo3", ctx.count("xo3")


def alloc_l1attn(ctx):
    ctx.alloc_sb("s_wr", [128, 16384], BF16)
    ctx.alloc_sb("s_bkv", [128, 24], F32)
    ctx.alloc_sb("s_tm", [128, 48], F32)
    ctx.alloc_sb("s_tdtf", [128, 16], F32)
    ctx.alloc_sb("s_bo", [128, 16], F32)
    ctx.alloc_sb("s_meo", [128, 2], F32)
    ctx.alloc_sb("s_kst", [128, 2048], F32)
    ctx.alloc_sb("s_vst", [128, 2048], F32)
    ctx.alloc_sb("s_qv", [128, 4096], BF16)
    ctx.alloc_sb("s_m8", [128, 48], F32)
    ctx.alloc_sb("s_cr", [128, 16], F32)
    ctx.alloc_sb("s_o5", [128, 1024], F32)
    ctx.alloc_sb("s_xt", [128, 2 * 1024], F32)
    ctx.alloc_sb("s_xo", [128, 2 * 1024], F32)


def build_l1attn_program(debug=False):
    with allow_ici_all_to_all():
        return _build_l1attn_program(debug)


def _build_l1attn_program(debug=False):
    nc = bass.Bass("TRN2", target_bir_lowering=False, debug=False)
    ap = {}
    ap["x_in"] = nc.dram_tensor("x_in", [2048, 1024], F32, kind="ExternalInput").ap()
    for nm, sh in (("wkT", [8, 16, 128, 128]), ("wvT", [8, 16, 128, 128]),
                   ("wrT", [8, 16, 128, 128]), ("woT", [16, 16, 128, 128])):
        ap[nm] = nc.dram_tensor(nm, sh, F32, kind="ExternalInput").ap()
    ap["zcol"] = nc.dram_tensor("zcol", [128, 16], BF16, kind="ExternalInput").ap()
    for nm, sh in (("bk", [128, 8]), ("bv", [128, 8]), ("br", [128, 8]),
                   ("bo", [128, 16]), ("tmk", [128, 16]), ("tmv", [128, 16]),
                   ("tmr", [128, 16]), ("td", [128, 8]), ("tf", [128, 8]),
                   ("m_eo", [128, 2])):
        ap[nm] = nc.dram_tensor(nm, sh, F32, kind="ExternalInput").ap()
    ap["x_out"] = nc.dram_tensor("x_out", [2048, 1024], F32, kind="ExternalOutput").ap()
    ap["ag3_i"] = nc.dram_tensor("ag3_i", [2048, 1024], BF16).ap()
    ap["ag3_o"] = nc.dram_tensor("ag3_o", [4096, 1024], BF16).ap()
    ap["rw_i"] = nc.dram_tensor("rw_i", [4096, 1024], BF16).ap()
    ap["rw_o"] = nc.dram_tensor("rw_o", [2048, 1024], BF16).ap()
    kind = "ExternalOutput" if debug else "Internal"
    ap["kbuf"] = nc.dram_tensor("kbuf", [1024, 2048], F32, kind=kind).ap()
    ap["vbuf"] = nc.dram_tensor("vbuf", [1024, 2048], BF16, kind=kind).ap()
    ap["srbuf"] = nc.dram_tensor("srbuf", [1024, 2048], BF16, kind=kind).ap()
    if debug:
        ap["agdbg"] = nc.dram_tensor("agdbg", [4096, 1024], BF16, kind="ExternalOutput").ap()
        ap["agdbg2"] = nc.dram_tensor("agdbg2", [2048, 1024], BF16, kind="ExternalOutput").ap()
        ap["rwdbg"] = nc.dram_tensor("rwdbg", [2048, 1024], BF16, kind="ExternalOutput").ap()
    ap["x_dram"] = ap["x_in"]

    es = contextlib.ExitStack()
    with es:
        ctx = Ctx(nc, es)
        alloc_common(ctx)
        alloc_l1attn(ctx)
        ctx.dma("sync", ctx.sb["sx"][:].rearrange("p (i t) -> p i t", i=16),
                ap["x_in"].rearrange("(i p) t -> p i t", p=128), "xld")
        ctx.wait("vector", "xld")
        ctx.wait("vector", "vinit")
        s, cnt = ph_l1attn(ctx, ap)
        ctx.wait("gpsimd", s, cnt)
        if debug:
            ctx.dma("gpsimd", ap["agdbg"], ap["ag3_o"], "dbgw")
            ctx.dma("gpsimd", ap["agdbg2"], ap["ag3_i"], "dbgw")
            ctx.dma("gpsimd", ap["rwdbg"], ap["rw_o"], "dbgw")
            ctx.wait("gpsimd", "dbgw")
        block = es.enter_context(nc.Block())
        ctx.finalize(block)
    return nc



def ph_l1ffn(ctx, ap):
    sx = ctx.sb["sx"]
    pbA = ctx.sb["pbA"]
    sxb = sx[:].bitcast(BF16)
    sxf = sx[:]

    lnsem, lncnt = emit_layernorm(ctx, sx, pbA, "ln4")
    ctx.wait("gpsimd", lnsem, lncnt)
    ctx.dma("gpsimd", ap["ag4_i"].rearrange("(i p) t -> p i t", p=128),
            pbA[:].rearrange("p (i t) -> p i t", i=16), "ag4w")
    ctx.wait("gpsimd", "ag4w")
    ctx.op("gpsimd", "collective_compute", "AllGather", ALU.bypass,
           replica_groups=ALL8, ins=[ap["ag4_i"].opt()], outs=[ap["ag4_o"].opt()],
           inc=(("ccs4", 1),))

    s_pk = ctx.sb["s_pk"]
    s_sc = ctx.sb["s_scq"]
    s_t1 = ctx.sb["s_t1"]
    s_t2 = ctx.sb["s_t2"]
    GU_W = 2 * IPC2
    ctx.dma("sync", s_pk.rearrange("p (h w) -> p h w", h=2),
            ap["gu_pkT"].rearrange("(h p) w -> p h w", p=128), "dqld")
    ctx.dma("sync", s_sc.bitcast(F16).rearrange("p (h w) -> p h w", h=2),
            ap["gu_scT"].rearrange("(h p) w -> p h w", p=128), "dqld")
    dq_t = ctx.count("dqld")
    ctx.wait("vector", "dqld", dq_t)
    ctx.wait("vector", lnsem, lncnt)
    for h2 in range(2):
        ctx.op("vector", "tensor_single_scalar",
               out=s_sc.bitcast(F16)[:, h2 * GU_W:(h2 + 1) * GU_W],
               in_=s_sc.bitcast(F16)[:, h2 * GU_W:(h2 + 1) * GU_W],
               scalar=float(2.0 ** 14), op=ALU.mult)
    for h2 in range(2):
        for j in range(8):
            kc = 2 * j + h2
            pk = s_pk.rearrange("p (h w) -> p h w", h=2)[:, h2]
            ctx.op("vector", "tensor_single_scalar", out=s_t1, in_=pk,
                   scalar=4 * j, op=ALU.logical_shift_right)
            ctx.op("vector", "tensor_scalar", out=s_t2, in0=s_t1,
                   scalar1=7, scalar2=25, op0=ALU.bitwise_and,
                   op1=ALU.logical_shift_left)
            ctx.op("vector", "tensor_scalar", out=s_t1, in0=s_t1,
                   scalar1=8, scalar2=28, op0=ALU.bitwise_and,
                   op1=ALU.logical_shift_left)
            ctx.op("vector", "tensor_tensor", out=s_t2, in0=s_t2, in1=s_t1,
                   op=ALU.bitwise_or)
            f16v = s_t2.bitcast(U16)[:, 1:2 * GU_W:2].bitcast(F16)
            ctx.op("vector", "tensor_tensor",
                   out=sxb[:, kc * GU_W:(kc + 1) * GU_W],
                   in0=f16v, in1=s_sc.bitcast(F16)[:, h2 * GU_W:(h2 + 1) * GU_W],
                   op=ALU.mult, inc=(("dqg", 1),))
    s_dpk = ctx.sb["s_dpk"]
    s_dsc = ctx.sb["s_dsc"]
    s_d1 = ctx.sb["s_d1"]
    s_d2 = ctx.sb["s_d2"]
    s_dd = ctx.sb["s_dd"]
    ctx.wait("sync", "dqg", 16)
    ctx.dma("sync", s_dpk, ap["dn_pkT"], "dqld")
    ctx.dma("sync", s_dsc.bitcast(F16), ap["dn_scT"], "dqld")
    ctx.wait("vector", "dqld", ctx.count("dqld"))
    ctx.op("vector", "tensor_single_scalar", out=s_dsc.bitcast(F16),
           in_=s_dsc.bitcast(F16), scalar=float(2.0 ** 14), op=ALU.mult)
    for j in range(8):
        ctx.op("vector", "tensor_single_scalar", out=s_d1, in_=s_dpk,
               scalar=4 * j, op=ALU.logical_shift_right)
        ctx.op("vector", "tensor_scalar", out=s_d2, in0=s_d1,
               scalar1=7, scalar2=25, op0=ALU.bitwise_and, op1=ALU.logical_shift_left)
        ctx.op("vector", "tensor_scalar", out=s_d1, in0=s_d1,
               scalar1=8, scalar2=28, op0=ALU.bitwise_and, op1=ALU.logical_shift_left)
        ctx.op("vector", "tensor_tensor", out=s_d2, in0=s_d2, in1=s_d1,
               op=ALU.bitwise_or)
        f16v = s_d2.bitcast(U16)[:, 1:2 * 2048:2].bitcast(F16)
        if j >= 1:
            ctx.wait("vector", "wdnw", 16 * j)
        ctx.op("vector", "tensor_tensor", out=s_dd, in0=f16v,
               in1=s_dsc.bitcast(F16), op=ALU.mult, inc=(("dqd", 1),))
        ctx.wait("gpsimd", "dqd")
        ctx.dma("gpsimd", ap["wdn_i"][j * IPP2:(j + 1) * IPP2, :], s_dd, "wdnw")
    ctx.wait("gpsimd", "wdnw")
    ctx.op("gpsimd", "collective_compute", "AllGather", ALU.bypass,
           replica_groups=ALL8, ins=[ap["wdn_i"].opt()], outs=[ap["wdn_o"].opt()],
           inc=(("ccs4", 1),))

    s_sg = ctx.sb["s_sg"]
    s_h2 = ctx.sb["s_h2st"]
    gu = lambda kc: sxb[:, kc * GU_W:(kc + 1) * GU_W]
    ctx.wait("tensor", "dqg", 16)
    ctx.wait("tensor", "ccs4", 1)
    ctx.wait("sync", "ccs4", 1)
    for g in range(16):
        blk, tl0 = g // 2, (g % 2) * 512
        slot = g % 2
        if g >= 2:
            ctx.wait("sync", "mmgu", ctx.count("mmgu"))
        ctx.dma("sync",
                pbA[:].rearrange("p (s k t) -> p s k t", s=2, k=16)[:, slot],
                ap["ag4_o"].rearrange("(b k p) t -> b p k t", b=8, p=128)[blk][:, :, tl0:tl0 + 512],
                "rhl")
        rhl_t = ctx.count("rhl")
        ctx.wait("tensor", "rhl", rhl_t)
        rhs = lambda kc: pbA[:].rearrange("p (s k t) -> p s k t", s=2, k=16)[:, slot, kc]
        ctx.wait("vector", "h2w", 16 * g)
        for m in range(6):
            bg = (2 * m) % 4
            bu = (2 * m + 1) % 4
            ctx.use_bank(bg)
            ctx.use_bank(bu)
            for kc in range(16):
                ctx.op("tensor", "matmul", ctx.psum[bg][:, 0:512],
                       gu(kc)[:, m * 128:(m + 1) * 128], rhs(kc),
                       start=(kc == 0), stop=(kc == 15))
            for kc in range(16):
                ctx.op("tensor", "matmul", ctx.psum[bu][:, 0:512],
                       gu(kc)[:, 768 + m * 128:768 + (m + 1) * 128], rhs(kc),
                       start=(kc == 0), stop=(kc == 15),
                       inc=((("mmgu", 1),) if kc == 15 else ()))
            ctx.wait("scalar", "mmgu")
            if ctx.count("h2e") >= 1:
                ctx.wait("scalar", "h2e", ctx.count("h2e") - 1)
            ctx.op("scalar", "activation", out=s_sg[:, (m % 2) * 512:(m % 2 + 1) * 512],
                   in_=ctx.psum[bg][:, 0:512], func=AF.Silu, inc=(("sil4", 1),))
            ctx.guard_bank(bg, "sil4")
            ctx.wait("vector", "sil4")
            ctx.op("vector", "tensor_mul", out=s_h2[:, m * 512:(m + 1) * 512],
                   in0=s_sg[:, (m % 2) * 512:(m % 2 + 1) * 512],
                   in1=ctx.psum[bu][:, 0:512], inc=(("h2e", 1),))
            ctx.guard_bank(bu, "h2e")
        ctx.wait("gpsimd", "h2e")
        ctx.dma("gpsimd",
                ap["h2d"].rearrange("(b m p) t -> b p m t", b=8, m=6)[blk][:, :, tl0:tl0 + 512],
                s_h2[:].rearrange("p (m t) -> p m t", m=6), "h2w")
    ctx.wait("gpsimd", "h2w")
    ctx.op("gpsimd", "collective_compute", "AllToAll", ALU.bypass,
           replica_groups=ALL8, ins=[ap["h2d"].opt()], outs=[ap["h2o"].opt()],
           inc=(("ccs4", 1),))

    s_wd = ctx.sb["s_wdst"]
    s_rhB = ctx.sb["s_rh2b"]
    s_xt = ctx.sb["s_xt"]
    s_xo = ctx.sb["s_xo"]
    sxf = sx[:]
    ctx.wait("sync", "ccs4", 3)

    def rhs_ap(kc, n):
        if kc < 32:
            return pbA[:, kc * 512:(kc + 1) * 512]
        return s_rhB[:, (kc - 32) * 512:(kc - 31) * 512]

    def wdld(ob, seq, wait=None):
        if wait is not None:
            ctx.wait("gpsimd", wait[0], wait[1])
        dst = s_wd.rearrange("p (s k m) -> p s k m", s=2, k=48)[:, seq % 2]
        ctx.dma("gpsimd", dst,
                ap["wdn_o"].rearrange("(k p) m -> p k m", p=128)[:, :, ob * 128:(ob + 1) * 128],
                "wdl")

    for ob in range(2):
        ctx.dma("sync", s_xt[:].rearrange("p (o t) -> p o t", o=2)[:, ob],
                ap["x_dram"].rearrange("(o p) t -> o p t", p=128)[ob], "xt4")
    seq = 0
    for n in range(2):
        ctx.wait("sync", "mmgu", ctx.count("mmgu"))
        if n > 0:
            ctx.wait("sync", "mmdn", ctx.count("mmdn"))
        ctx.dma("sync", pbA[:].rearrange("p (k t) -> p k t", k=32),
                ap["h2o"].rearrange("(k p) t -> p k t", p=128)[:, 0:32, n * 512:(n + 1) * 512],
                "rh2l")
        ctx.dma("sync", s_rhB.rearrange("p (k t) -> p k t", k=16),
                ap["h2o"].rearrange("(k p) t -> p k t", p=128)[:, 32:48, n * 512:(n + 1) * 512],
                "rh2l")
        rh2_t = ctx.count("rh2l")
        if n == 0:
            wdld(0, seq, wait=("ccs4", 2))
            wdld(1, seq + 1)
        ctx.wait("tensor", "rh2l", rh2_t)
        for ob in range(16):
            ctx.wait("tensor", "wdl", 16 * (seq + 1))
            bank = (2 * ob + n) % 4
            ctx.use_bank(bank)
            for kc in range(48):
                ctx.op("tensor", "matmul", ctx.psum[bank][:, 0:512],
                       s_wd.rearrange("p (s k m) -> p s k m", s=2, k=48)[:, seq % 2, kc],
                       rhs_ap(kc, n),
                       start=(kc == 0), stop=(kc == 47),
                       inc=((("mmdn", 1),) if kc == 47 else ()))
            if seq + 2 < 32:
                wdld((ob + 2) % 16, seq + 2, wait=("mmdn", ctx.count("mmdn")))
            ctx.wait("vector", "mmdn")
            ctx.op("vector", "tensor_copy",
                   out=sxf[:, ob * 1024 + n * 512: ob * 1024 + (n + 1) * 512],
                   in_=ctx.psum[bank][:, 0:512], inc=(("yev", 1),))
            ctx.guard_bank(bank, "yev")
            seq += 1
    s_sq = ctx.sb["s_sq"]
    sqbf = s_sq[:].bitcast(BF16)
    s_stat = ctx.sb["s_stat"]
    s_rstd = ctx.sb["s_rstd"]
    ctx.wait("scalar", "yev")
    for i in range(16):
        if i >= 2:
            ctx.wait("scalar", "rsqc", i - 1)
        ctx.op("scalar", "activation", out=sqbf[:, (i % 2) * 1024:(i % 2 + 1) * 1024],
               in_=sxf[:, i * 1024:(i + 1) * 1024], func=AF.Square,
               inc=(("rsq", 1),))
    ctx.wait("tensor", "yev")
    for bk in range(4):
        ctx.use_bank(bk)
    for i in range(16):
        ctx.wait("tensor", "rsq", i + 1)
        for n in range(2):
            last = i == 15
            inc = ()
            if n == 1:
                inc = (("rsmm", 1),) if last else (("rsqc", 1),)
            ctx.op("tensor", "matmul", ctx.psum[n][0:1, 0:512],
                   ctx.sb["ones_col16"][:, 0:1],
                   sqbf[:, (i % 2) * 1024 + n * 512:(i % 2) * 1024 + (n + 1) * 512],
                   start=(i == 0), stop=last, inc=inc)
    ctx.wait("vector", "rsmm")
    for n in range(2):
        ctx.op("vector", "tensor_copy", out=s_stat[0:1, n * 512:(n + 1) * 512],
               in_=ctx.psum[n][0:1, 0:512], inc=(("rsev", 1),))
    ctx.wait("tensor", "rsev")
    for n in range(2):
        ctx.op("tensor", "matmul", ctx.psum[2 + n][:, 0:512],
               ctx.sb["ones_row"][0:1, 0:128],
               s_stat[0:1, n * 512:(n + 1) * 512], start=True, stop=True,
               inc=((("rsbc", 1),) if n == 1 else ()))
    ctx.wait("scalar", "rsbc")
    for n in range(2):
        ctx.op("scalar", "activation", out=s_sq[:, n * 512:(n + 1) * 512],
               in_=ctx.psum[2 + n][:, 0:512], func=AF.Ln,
               bias=ctx.sb["s_eps6"][:, 0:1], scale=1.0 / 2048.0)
    ctx.op("scalar", "activation", out=s_rstd[:, 0:1024], in_=s_sq[:, 0:1024],
           func=AF.Exp, bias=0.0, scale=-0.5, inc=(("rsrs", 1),))
    for bk in range(4):
        ctx.guard_bank(bk, "rsrs")
    ctx.wait("vector", "rsrs")
    for ob in range(16):
        ctx.wait("vector", "xt4", 16 * (ob + 1))
        if ob >= 2:
            ctx.wait("vector", "xo4", 16 * (ob - 1))
        ctx.op("vector", "tensor_mul", out=sxf[:, ob * 1024:(ob + 1) * 1024],
               in0=sxf[:, ob * 1024:(ob + 1) * 1024], in1=s_rstd[:, 0:1024].to_broadcast((128, 1024)) if False else s_rstd[:, 0:1024])
        ctx.op("vector", "tensor_add",
               out=s_xo[:, (ob % 2) * 1024:(ob % 2 + 1) * 1024],
               in0=sxf[:, ob * 1024:(ob + 1) * 1024],
               in1=s_xt[:, (ob % 2) * 1024:(ob % 2 + 1) * 1024], inc=(("fout", 1),))
        ctx.wait("gpsimd", "fout")
        ctx.dma("gpsimd", ap["x_out"].rearrange("(o p) t -> o p t", p=128)[ob],
                s_xo[:].rearrange("p (o t) -> p o t", o=2)[:, ob % 2], "xo4")
        if ob + 2 < 16:
            ctx.wait("sync", "fout")
            ctx.dma("sync", s_xt[:].rearrange("p (o t) -> p o t", o=2)[:, ob % 2],
                    ap["x_dram"].rearrange("(o p) t -> o p t", p=128)[ob + 2], "xt4")
    return "xo4", ctx.count("xo4")


def alloc_l1ffn(ctx):
    u = ctx.alloc_sb("u_pool", [128, 24576], BF16)
    ub = u[:]
    ctx.sb["s_pk"] = ub[:, 0:6144].bitcast(I32)
    ctx.sb["s_scq"] = ub[:, 6144:9216].bitcast(U16)
    ctx.sb["s_t1"] = ub[:, 9216:12288].bitcast(I32)
    ctx.sb["s_t2"] = ub[:, 12288:15360].bitcast(I32)
    ctx.sb["s_dpk"] = ub[:, 0:4096].bitcast(I32)[0:96, :]
    ctx.sb["s_dsc"] = ub[:, 4096:6144].bitcast(U16)[0:96, :]
    ctx.sb["s_d1"] = ub[:, 6144:10240].bitcast(I32)[0:96, :]
    ctx.sb["s_d2"] = ub[:, 10240:14336].bitcast(I32)[0:96, :]
    ctx.sb["s_dd"] = ub[:, 14336:16384][0:96, :]
    ctx.sb["s_wdst"] = ub[:, 0:12288]
    ctx.sb["s_rh2b"] = ub[:, 12288:20480]
    ctx.alloc_sb("s_sg", [128, 2 * 512], F32)
    ctx.alloc_sb("s_h2st", [128, 6 * 512], BF16)
    ctx.alloc_sb("s_xt", [128, 2 * 1024], F32)
    ctx.alloc_sb("s_xo", [128, 2 * 1024], F32)


def build_l1ffn_program():
    with allow_ici_all_to_all():
        return _build_l1ffn_program()


def _build_l1ffn_program():
    nc = bass.Bass("TRN2", target_bir_lowering=False, debug=False)
    ap = {}
    ap["x_in"] = nc.dram_tensor("x_in", [2048, 1024], F32, kind="ExternalInput").ap()
    ap["gu_pkT"] = nc.dram_tensor("gu_pkT", [256, 1536], I32, kind="ExternalInput").ap()
    ap["gu_scT"] = nc.dram_tensor("gu_scT", [256, 1536], F16, kind="ExternalInput").ap()
    ap["dn_pkT"] = nc.dram_tensor("dn_pkT", [96, 2048], I32, kind="ExternalInput").ap()
    ap["dn_scT"] = nc.dram_tensor("dn_scT", [96, 2048], F16, kind="ExternalInput").ap()
    ap["x_out"] = nc.dram_tensor("x_out", [2048, 1024], F32, kind="ExternalOutput").ap()
    ap["ag4_i"] = nc.dram_tensor("ag4_i", [2048, 1024], BF16).ap()
    ap["ag4_o"] = nc.dram_tensor("ag4_o", [16384, 1024], BF16).ap()
    ap["wdn_i"] = nc.dram_tensor("wdn_i", [768, 2048], BF16).ap()
    ap["wdn_o"] = nc.dram_tensor("wdn_o", [6144, 2048], BF16).ap()
    ap["h2d"] = nc.dram_tensor("h2d", [6144, 1024], BF16).ap()
    ap["h2o"] = nc.dram_tensor("h2o", [6144, 1024], BF16).ap()
    ap["x_dram"] = ap["x_in"]

    es = contextlib.ExitStack()
    with es:
        ctx = Ctx(nc, es)
        alloc_common(ctx)
        alloc_l1ffn(ctx)
        ctx.dma("sync", ctx.sb["sx"][:].rearrange("p (i t) -> p i t", i=16),
                ap["x_in"].rearrange("(i p) t -> p i t", p=128), "xld")
        ctx.wait("vector", "xld")
        ctx.wait("vector", "vinit")
        s, cnt = ph_l1ffn(ctx, ap)
        ctx.wait("gpsimd", s, cnt)
        block = es.enter_context(nc.Block())
        ctx.finalize(block)
    return nc
